# version 01
# baseline (speedup 1.0000x reference)
import numpy as np
import concourse.bass as bass
import concourse.mybir as mybir
from concourse.bass_utils import run_bass_kernel_spmd

F32 = mybir.dt.float32
BF16 = mybir.dt.bfloat16
AF = mybir.ActivationFunctionType
ALU = mybir.AluOpType
AX = mybir.AxisListType.X

T, C, F, L = 4096, 1024, 4096, 4
NCB = C // 128
NFB = F // 128
SA, SB = 512, 256
EPS = 1e-5

ENGINES = ("tensor", "vector", "scalar", "gpsimd", "sync")


class Sched:

    def __init__(self, nc):
        self.nc = nc
        self.streams = {e: [] for e in ENGINES}
        self.sems = {e: nc.alloc_semaphore(f"sem_{e}") for e in ENGINES}
        self.state = {}
        self.hw = {(a, b): -1 for a in ENGINES for b in ENGINES}

    @staticmethod
    def _key(b):
        return b if isinstance(b, (str, tuple, int)) else id(b)

    def _st(self, key):
        return self.state.setdefault(self._key(key), {"w": None, "r": []})

    SELF_SYNC = ("vector", "scalar")

    def op(self, eng, fn, reads=(), writes=(), dma=False, tag=""):
        idx = len(self.streams[eng])
        selfsync = eng in self.SELF_SYNC
        waits = {}
        for b in reads:
            st = self._st(b)
            if st["w"] is not None:
                we, wi = st["w"]
                if we != eng or selfsync:
                    waits[we] = max(waits.get(we, -1), wi)
        for b in writes:
            st = self._st(b)
            if st["w"] is not None:
                we, wi = st["w"]
                if we != eng or selfsync:
                    waits[we] = max(waits.get(we, -1), wi)
            for re_, ri in st["r"]:
                if re_ != eng or selfsync:
                    waits[re_] = max(waits.get(re_, -1), ri)
        final_waits = []
        for we, wi in waits.items():
            if self.hw[(eng, we)] >= wi:
                continue
            self.hw[(eng, we)] = wi
            self.streams[we][wi]["inc"] = True
            final_waits.append((we, wi))
        self.streams[eng].append(
            {"fn": fn, "waits": final_waits, "inc": dma, "dma": dma,
             "tag": tag})
        for b in writes:
            st = self._st(b)
            st["w"] = (eng, idx)
            st["r"] = []
        for b in reads:
            self._st(b)["r"].append((eng, idx))
        return idx

    def alias(self, old_keys, new_keys):
        pts = []
        for k in old_keys:
            st = self._st(k)
            if st["w"] is not None:
                pts.append(st["w"])
            pts.extend(st["r"])
        for k in new_keys:
            st = self._st(k)
            st["r"].extend(pts)

    def finalize(self):
        self.counts = {}
        self.totals = {}
        for e in ENGINES:
            c = 0
            for i, ent in enumerate(self.streams[e]):
                if ent["inc"]:
                    c += 16 if ent["dma"] else 1
                self.counts[(e, i)] = c
            self.totals[e] = c

    def emit(self, eng_name, eng):
        c = 0
        for ent in self.streams[eng_name]:
            for we, wi in ent["waits"]:
                eng.wait_ge(self.sems[we], self.counts[(we, wi)])
            if ent["dma"] and c > 0:
                eng.wait_ge(self.sems[eng_name], c)
            inst = ent["fn"](eng)
            if ent["inc"]:
                inst.then_inc(self.sems[eng_name], 16 if ent["dma"] else 1)
            if ent["inc"]:
                c += 16 if ent["dma"] else 1


class Alloc:
    def __init__(self, nc, base=16512, budget=229344):
        self.nc = nc
        self.off = base
        self.budget = budget

    def t(self, name, shape, dtype, at=None):
        per_part = int(np.prod(shape[1:])) * (2 if dtype == BF16 else 4)
        per_part = (per_part + 127) // 128 * 128
        if at is None:
            at = self.off
            self.off = at + per_part
        assert at + per_part <= self.budget, (
            f"SBUF overflow: {name} at {at}+{per_part} > {self.budget}")
        return self.nc.alloc_sbuf_tensor_at(name, list(shape), dtype,
                                            offset=at)


def build_nc(n_layers=L, with_ffn=True, dump=None):
    nc = bass.Bass("TRN2", target_bir_lowering=False, debug=False)

    x0 = nc.dram_tensor("x0", [T, C], F32, kind="ExternalInput").ap()
    idf_in = nc.dram_tensor("idf", [128, 128], F32, kind="ExternalInput").ap()
    wT = {}
    for l in range(L):
        for nm, shape in (("wk", [C, C]), ("wv", [C, C]), ("wr", [C, C]),
                          ("wo", [C, C]), ("wck", [C, F]), ("wcv", [F, C]),
                          ("wcr", [C, C])):
            wT[(nm, l)] = nc.dram_tensor(f"{nm}T{l}", shape, F32,
                                         kind="ExternalInput").ap()
    pv = {}
    for l in range(L):
        for nm in ("tmk", "tmv", "tmr", "cmk", "cmr", "td", "tf",
                   "ln1w", "ln1b", "ln2w", "ln2b"):
            pv[(nm, l)] = nc.dram_tensor(f"{nm}{l}", [128, NCB], F32,
                                         kind="ExternalInput").ap()
    ln0w_d = nc.dram_tensor("ln0w", [1, C], F32, kind="ExternalInput").ap()
    ln0b_d = nc.dram_tensor("ln0b", [1, C], F32, kind="ExternalInput").ap()
    w1_d = nc.dram_tensor("w1", [C, F], F32, kind="ExternalInput").ap()
    b1_d = nc.dram_tensor("b1", [128, NFB], F32, kind="ExternalInput").ap()
    w2_d = nc.dram_tensor("w2", [F, C], F32, kind="ExternalInput").ap()
    b2_d = nc.dram_tensor("b2", [1, C], F32, kind="ExternalInput").ap()
    out_d = nc.dram_tensor("out", [T, C], F32, kind="ExternalOutput").ap()
    bufA = nc.dram_tensor("xbufA", [T, C], F32, kind="Internal").ap()
    bufB = nc.dram_tensor("xbufB", [T, C], F32, kind="Internal").ap()
    wcv_bf = [nc.dram_tensor(f"wcv_bf{l}", [F, C], BF16,
                             kind="Internal").ap() for l in range(L)]
    w2_bf = nc.dram_tensor("w2_bf", [F, C], BF16, kind="Internal").ap()

    al = Alloc(nc)
    x_tm = al.t("x_tm", [128, SA // 128, C], F32)
    out_scr = [al.t(f"out_scr{i}", [128, C], F32) for i in range(2)]
    xn = [al.t(f"xn{i}", [128, C], F32) for i in range(4)]
    h_cb = [al.t(f"h{i}", [128, SA + 1], F32) for i in range(NCB)]
    xk_cb = [al.t(f"xk{i}", [128, SA], BF16) for i in range(NCB)]
    xr_cb = [al.t(f"xr{i}", [128, SA], BF16) for i in range(NCB)]
    ssum = al.t("ssum", [128, 8], F32)
    ssq = al.t("ssq", [128, 8], F32)
    smean = al.t("smean", [128, 8], F32)
    svar = al.t("svar", [128, 8], F32)
    srstd = al.t("srstd", [128, 8], F32)
    cst = al.t("cst", [128, 2], F32)
    acar = al.t("acar", [128, NCB], F32)
    bcar = al.t("bcar", [128, NCB], F32)
    par = {}
    for l in range(n_layers):
        for nm in ("tmk", "tmv", "tmr", "cmk", "cmr", "ln1w", "ln1b",
                   "ln2w", "ln2b", "ew", "eu"):
            par[(nm, l)] = al.t(f"p_{nm}{l}", [128, NCB], F32)
    b1_sb = al.t("b1_sb", [128, NFB], F32)
    idf = al.t("idf_sb", [128, 128], F32)
    ones_col = al.t("ones_col", [1, 128], F32)

    rw1 = al.off
    wk_sb = al.t("wk_sb", [128, NCB, NCB, 128], BF16, at=rw1)
    wv_sb = al.t("wv_sb", [128, NCB, NCB, 128], BF16, at=rw1 + 16384)
    wr_sb = al.t("wr_sb", [128, NCB, NCB, 128], BF16, at=rw1 + 32768)
    wo_sb = al.t("wo_sb", [128, NCB, C], BF16, at=rw1 + 49152)
    wck_sb = al.t("wck_sb", [128, NCB, NFB, 128], BF16, at=rw1)
    al.off = rw1 + 65536
    RW1_A = [wk_sb, wv_sb, wr_sb, wo_sb]
    RW1_B = [wck_sb]

    rw2 = al.off
    o = rw2
    xv_cb = []
    for i in range(NCB):
        xv_cb.append(al.t(f"xv{i}", [128, SA], BF16, at=o)); o += 1024
    rwkv_cb = []
    for i in range(NCB):
        rwkv_cb.append(al.t(f"rwkv{i}", [128, SA], BF16, at=o)); o += 1024
    ek_s, ekv_s, msc_s, a_s, b_s, dec_s = [], [], [], [], [], []
    for i in range(2):
        ek_s.append(al.t(f"ek{i}", [128, SA], F32, at=o)); o += 2048
        ekv_s.append(al.t(f"ekv{i}", [128, SA], F32, at=o)); o += 2048
        msc_s.append(al.t(f"msc{i}", [128, SA], F32, at=o)); o += 2048
        a_s.append(al.t(f"a{i}", [128, SA + 1], F32, at=o)); o += 2176
        b_s.append(al.t(f"b{i}", [128, SA + 1], F32, at=o)); o += 2176
        dec_s.append(al.t(f"dec{i}", [128, SA], F32, at=o)); o += 2048
    er_s = al.t("er", [128, SA], F32, at=o); o += 2048
    num_s = al.t("num", [128, SA], F32, at=o); o += 2048
    den_s = al.t("den", [128, SA], F32, at=o); o += 2048
    rw2_a_end = o
    RW2_A = (xv_cb + rwkv_cb + ek_s + ekv_s + msc_s + a_s + b_s + dec_s
             + [er_s, num_s, den_s])
    o = rw2
    wcvb_s = []
    for i in range(2):
        wcvb_s.append(al.t(f"wcvb{i}", [128, 4, C], BF16, at=o)); o += 8192
    gj_s = []
    for i in range(2):
        gj_s.append(al.t(f"gj{i}", [128, C], F32, at=o)); o += 4096
    kf_s = []
    for i in range(2):
        kf_s.append(al.t(f"kf{i}", [128, SB], BF16, at=o)); o += 512
    rl_s = al.t("rl", [128, SB], F32, at=o); o += 1024
    xaux_tm = al.t("xaux", [128, SB // 128, C], F32, at=o); o += 8192
    dscB_s = []
    for i in range(2):
        dscB_s.append(al.t(f"dscB{i}", [128, SB], F32, at=o)); o += 1024
    rw2_b_end = o
    RW2_B = wcvb_s + gj_s + kf_s + [rl_s, xaux_tm] + dscB_s
    al.off = rw2 + max(rw2_a_end - rw2, rw2_b_end - rw2)

    rw3 = al.off
    wcr_sb = al.t("wcr_sb", [128, NCB, C], BF16, at=rw3)
    w0b = al.t("w0b", [128, C], F32, at=rw3)
    b0b = al.t("b0b", [128, C], F32, at=rw3 + 4096)
    rowbuf = al.t("rowbuf", [1, C], F32, at=rw3 + 8192)
    b2b = al.t("b2b", [128, C], F32, at=rw3)
    al.off = rw3 + 16384
    RW3_P = [w0b, b0b, rowbuf]
    RW3_B = [wcr_sb]
    RW3_F = [b2b]

    sqjunk = out_scr[0]

    P512 = [nc.alloc_psum_tensor(f"P512_{i}", [128, 512], F32)
            for i in range(4)]
    P1024 = [nc.alloc_psum_tensor(f"P1024_{i}", [128, 1024], F32)
             for i in range(2)]

    sch = Sched(nc)

    def V(fn, r=(), w=(), tag=""):
        sch.op("vector", fn, r, w, tag=tag)

    def S(fn, r=(), w=(), tag=""):
        sch.op("scalar", fn, r, w, tag=tag)

    def TE(fn, r=(), w=(), tag=""):
        sch.op("tensor", fn, r, w, tag=tag)

    def DMAW(fn, r=(), w=(), tag=""):
        sch.op("gpsimd", fn, r, w, dma=True, tag=tag)

    def DMAX(fn, r=(), w=(), tag=""):
        sch.op("sync", fn, r, w, dma=True, tag=tag)

    XD = {id(x0.tensor): "x0", id(bufA.tensor): "bufA",
          id(bufB.tensor): "bufB"}

    def dkey(ap):
        return XD.get(id(ap.tensor), ap.tensor)

    for l in range(n_layers):
        for nm in ("tmk", "tmv", "tmr", "cmk", "cmr", "ln1w", "ln1b",
                   "ln2w", "ln2b"):
            DMAX(lambda e, d=par[(nm, l)], s=pv[(nm, l)]: e.dma_start(
                d[:], s), w=[par[(nm, l)]], tag=f"p_{nm}{l}")
        DMAX(lambda e, d=par[("ew", l)], s=pv[("td", l)]: e.dma_start(
            d[:], s), w=[par[("ew", l)]], tag=f"p_td{l}")
        S(lambda e, p=par[("ew", l)]: e.activation(p[:], p[:], AF.Exp),
          r=[par[("ew", l)]], w=[par[("ew", l)]], tag=f"ew1_{l}")
        S(lambda e, p=par[("ew", l)]: e.activation(p[:], p[:], AF.Exp,
                                                   scale=-1.0),
          r=[par[("ew", l)]], w=[par[("ew", l)]], tag=f"ew2_{l}")
        DMAX(lambda e, d=par[("eu", l)], s=pv[("tf", l)]: e.dma_start(
            d[:], s), w=[par[("eu", l)]], tag=f"p_tf{l}")
        S(lambda e, p=par[("eu", l)]: e.activation(p[:], p[:], AF.Exp),
          r=[par[("eu", l)]], w=[par[("eu", l)]], tag=f"eu_{l}")
    DMAX(lambda e: e.dma_start(b1_sb[:], b1_d), w=[b1_sb], tag="b1")
    DMAX(lambda e: e.dma_start(idf[:], idf_in), w=[idf], tag="idf")
    V(lambda v: v.memset(ones_col[:], 1.0), w=[ones_col], tag="ones")
    V(lambda v: v.memset(cst[:, 0:1], EPS), w=[cst], tag="c_eps")
    V(lambda v: v.memset(cst[:, 1:2], 1.0), w=[cst], tag="c_one")

    def bcast_row(src_row, dst, nm):
        DMAX(lambda e, s=src_row: e.dma_start(rowbuf[:], s),
             w=[rowbuf], tag=f"row_{nm}")
        for half in range(2):
            TE(lambda t, half=half: t.matmul(
                P1024[0][:, half * 512:(half + 1) * 512], ones_col[:],
                rowbuf[0:1, half * 512:(half + 1) * 512],
                start=True, stop=True),
               r=[ones_col, rowbuf], w=[P1024[0]], tag=f"bc_{nm}{half}")
        V(lambda v, d=dst: v.tensor_copy(d[:], P1024[0][:]),
          r=[P1024[0]], w=[dst], tag=f"bcc_{nm}")

    if n_layers > 0:
        bcast_row(ln0w_d, w0b, "w0b")
        bcast_row(ln0b_d, b0b, "b0b")
    for l in range(n_layers):
        DMAW(lambda g, l=l: g.dma_start(wcv_bf[l][:, :], wT[("wcv", l)]),
             w=[("wcv_bf", l)], tag=f"dump_wcv{l}")
    if with_ffn:
        DMAW(lambda g: g.dma_start(w2_bf[:, :], w2_d),
             w=[("w2_bf",)], tag="dump_w2")

    def load_weights_A(l):
        for wsb, key in ((wk_sb, "wk"), (wv_sb, "wv"), (wr_sb, "wr")):
            DMAW(lambda g, d=wsb, s=wT[(key, l)]: g.dma_start(
                d[:], s.rearrange("(a p) (b j) -> p a b j", p=128, j=128)),
                 w=[wsb], tag=f"ldA_{key}{l}")
        DMAW(lambda g, s=wT[("wo", l)]: g.dma_start(
            wo_sb[:], s.rearrange("(ob p) c -> p ob c", p=128)),
             w=[wo_sb], tag=f"ldA_wo{l}")

    def load_weights_B(l):
        DMAW(lambda g, s=wT[("wck", l)]: g.dma_start(
            wck_sb[:], s.rearrange("(a p) (b j) -> p a b j", p=128, j=128)),
             w=[wck_sb], tag=f"ldB_wck{l}")
        DMAW(lambda g, s=wT[("wcr", l)]: g.dma_start(
            wcr_sb[:], s.rearrange("(ci p) c -> p ci c", p=128)),
             w=[wcr_sb], tag=f"ldB_wcr{l}")

    def load_weights_F():
        DMAW(lambda g: g.dma_start(
            wck_sb[:], w1_d.rearrange("(a p) (b j) -> p a b j", p=128,
                                      j=128)),
             w=[wck_sb], tag="ldF_w1")

    def stats_ln(tiles, keys, ntb, col0, tag):
        for tb in range(ntb):
            S(lambda e, a=tiles[tb], c=col0 + tb: e.activation(
                sqjunk[:], a, AF.Square, accum_out=ssq[:, c:c + 1]),
              r=[keys[tb]], w=[sqjunk, ssq], tag=f"{tag}_sq{tb}")
            V(lambda v, a=tiles[tb], c=col0 + tb: v.reduce_sum(
                ssum[:, c:c + 1], a, axis=AX),
              r=[keys[tb]], w=[ssum], tag=f"{tag}_sm{tb}")
        sl = slice(col0, col0 + ntb)
        V(lambda v: v.tensor_scalar(smean[:, sl], ssum[:, sl], 1.0 / C,
                                    None, ALU.mult),
          r=[ssum], w=[smean], tag=f"{tag}_mean")
        V(lambda v: v.tensor_scalar(svar[:, sl], ssq[:, sl], 1.0 / C,
                                    None, ALU.mult),
          r=[ssq], w=[svar], tag=f"{tag}_v1")
        V(lambda v: v.tensor_mul(ssq[:, sl], smean[:, sl], smean[:, sl]),
          r=[smean], w=[ssq], tag=f"{tag}_msq")
        V(lambda v: v.tensor_sub(svar[:, sl], svar[:, sl], ssq[:, sl]),
          r=[ssq, svar], w=[svar], tag=f"{tag}_v2")
        S(lambda e: e.activation(svar[:, sl], svar[:, sl], AF.Ln,
                                 bias=cst[:, 0:1]),
          r=[svar, cst], w=[svar], tag=f"{tag}_ln")
        S(lambda e: e.activation(srstd[:, sl], svar[:, sl], AF.Exp,
                                 scale=-0.5),
          r=[svar], w=[srstd], tag=f"{tag}_ex")

    def phase_A(l, src):
        ntb = SA // 128
        prologue = (l == 0)
        V(lambda v: v.memset(acar[:], 0.0), w=[acar], tag=f"A{l}_ac0")
        V(lambda v: v.memset(bcar[:], 0.0), w=[bcar], tag=f"A{l}_bc0")
        for cb in range(NCB):
            V(lambda v, cb=cb: v.memset(h_cb[cb][:, 0:1], 0.0),
              w=[h_cb[cb]], tag=f"A{l}_h0_{cb}")
        for s in range(T // SA):
            t0 = s * SA
            DMAX(lambda e, t0=t0, src=src: e.dma_start(
                x_tm[:], src[t0:t0 + SA, :].rearrange(
                    "(tb p) c -> p tb c", p=128)),
                 r=[dkey(src)], w=[x_tm], tag=f"A{l}s{s}_ldx")
            xt = [x_tm[:, tb, :] for tb in range(ntb)]
            xtk = [x_tm] * ntb
            if prologue:
                for rep in range(2):
                    stats_ln(xt, xtk, ntb, 0, f"A{l}s{s}_pr{rep}")
                    for tb in range(ntb):
                        V(lambda v, tb=tb: v.tensor_scalar(
                            x_tm[:, tb, :], x_tm[:, tb, :],
                            smean[:, tb:tb + 1], srstd[:, tb:tb + 1],
                            ALU.subtract, ALU.mult),
                          r=[x_tm, smean, srstd], w=[x_tm],
                          tag=f"A{l}s{s}_pr{rep}n{tb}")
                for tb in range(ntb):
                    V(lambda v, tb=tb: v.tensor_mul(
                        x_tm[:, tb, :], x_tm[:, tb, :], w0b[:]),
                      r=[x_tm, w0b], w=[x_tm], tag=f"A{l}s{s}_pw{tb}")
                    V(lambda v, tb=tb: v.tensor_add(
                        x_tm[:, tb, :], x_tm[:, tb, :], b0b[:]),
                      r=[x_tm, b0b], w=[x_tm], tag=f"A{l}s{s}_pb{tb}")
            stats_ln(xt, xtk, ntb, 0, f"A{l}s{s}_l1")
            for tb in range(ntb):
                V(lambda v, tb=tb: v.tensor_scalar(
                    xn[tb][:], x_tm[:, tb, :], smean[:, tb:tb + 1],
                    srstd[:, tb:tb + 1], ALU.subtract, ALU.mult),
                  r=[x_tm, smean, srstd], w=[xn[tb]], tag=f"A{l}s{s}_xn{tb}")
            for cb in range(NCB):
                pt = P512[cb % 2]
                for tb in range(ntb):
                    TE(lambda t, tb=tb, cb=cb, pt=pt: t.transpose(
                        pt[:, tb * 128:(tb + 1) * 128],
                        xn[tb][:, cb * 128:(cb + 1) * 128], idf[:]),
                       r=[xn[tb], idf], w=[pt], tag=f"A{l}s{s}_tr{cb}_{tb}")
                S(lambda e, cb=cb, pt=pt: e.activation(
                    h_cb[cb][:, 1:SA + 1], pt[:, 0:SA], AF.Identity,
                    bias=par[("ln1b", l)][:, cb:cb + 1],
                    scale=par[("ln1w", l)][:, cb:cb + 1]),
                  r=[pt, par[("ln1w", l)], par[("ln1b", l)]],
                  w=[h_cb[cb]], tag=f"A{l}s{s}_ev{cb}")
                dsc = msc_s[cb % 2]
                V(lambda v, cb=cb, dsc=dsc: v.tensor_sub(
                    dsc[:], h_cb[cb][:, 1:SA + 1], h_cb[cb][:, 0:SA]),
                  r=[h_cb[cb]], w=[dsc], tag=f"A{l}s{s}_d{cb}")
                for nm, dst in (("tmk", xk_cb), ("tmv", xv_cb),
                                ("tmr", xr_cb)):
                    V(lambda v, cb=cb, nm=nm, dst=dst, dsc=dsc:
                      v.scalar_tensor_tensor(
                          dst[cb][:], dsc[:], par[(nm, l)][:, cb:cb + 1],
                          h_cb[cb][:, 0:SA], ALU.mult, ALU.add),
                      r=[dsc, h_cb[cb], par[(nm, l)]], w=[dst[cb]],
                      tag=f"A{l}s{s}_m{nm}{cb}")
                V(lambda v, cb=cb: v.tensor_copy(h_cb[cb][:, 0:1],
                                                 h_cb[cb][:, SA:SA + 1]),
                  r=[h_cb[cb]], w=[h_cb[cb]], tag=f"A{l}s{s}_bnd{cb}")
            for cb in range(NCB):
                for ci in range(NCB):
                    TE(lambda t, cb=cb, ci=ci: t.matmul(
                        P512[2][:], wk_sb[:, ci, cb, :], xk_cb[ci][:],
                        start=(ci == 0), stop=(ci == NCB - 1)),
                       r=[wk_sb, xk_cb[ci]], w=[P512[2]],
                       tag=f"A{l}s{s}_mmk{cb}_{ci}")
                for ci in range(NCB):
                    TE(lambda t, cb=cb, ci=ci: t.matmul(
                        P512[3][:], wv_sb[:, ci, cb, :], xv_cb[ci][:],
                        start=(ci == 0), stop=(ci == NCB - 1)),
                       r=[wv_sb, xv_cb[ci]], w=[P512[3]],
                       tag=f"A{l}s{s}_mmv{cb}_{ci}")
                for ci in range(NCB):
                    TE(lambda t, cb=cb, ci=ci: t.matmul(
                        P1024[0][:, 0:SA], wr_sb[:, ci, cb, :],
                        xr_cb[ci][:],
                        start=(ci == 0), stop=(ci == NCB - 1)),
                       r=[wr_sb, xr_cb[ci]], w=[P1024[0]],
                       tag=f"A{l}s{s}_mmr{cb}_{ci}")
                pi = cb % 2
                ek, ekv, msc = ek_s[pi], ekv_s[pi], msc_s[pi]
                a_, b_, dec = a_s[pi], b_s[pi], dec_s[pi]
                S(lambda e, ek=ek: e.activation(ek[:], P512[2][:], AF.Exp),
                  r=[P512[2]], w=[ek], tag=f"A{l}s{s}_ek{cb}")
                S(lambda e: e.activation(er_s[:], P1024[0][:, 0:SA],
                                         AF.Exp, scale=-1.0),
                  r=[P1024[0]], w=[er_s], tag=f"A{l}s{s}_er{cb}")
                V(lambda v, ekv=ekv, ek=ek: v.tensor_mul(
                    ekv[:], ek[:], P512[3][:]),
                  r=[ek, P512[3]], w=[ekv], tag=f"A{l}s{s}_ekv{cb}")
                V(lambda v, dec=dec: v.memset(dec[:], 1.0), w=[dec],
                  tag=f"A{l}s{s}_dc1{cb}")
                V(lambda v, dec=dec, cb=cb: v.tensor_scalar(
                    dec[:], dec[:], par[("ew", l)][:, cb:cb + 1], None,
                    ALU.mult),
                  r=[par[("ew", l)], dec], w=[dec], tag=f"A{l}s{s}_dc2{cb}")
                V(lambda v, a_=a_, cb=cb: v.tensor_copy(
                    a_[:, 0:1], acar[:, cb:cb + 1]),
                  r=[acar], w=[a_], tag=f"A{l}s{s}_ac{cb}")
                V(lambda v, a_=a_, dec=dec, ekv=ekv, cb=cb:
                  v.tensor_tensor_scan(a_[:, 1:SA + 1], dec[:], ekv[:],
                                       acar[:, cb:cb + 1], ALU.mult,
                                       ALU.add),
                  r=[dec, ekv, acar], w=[a_], tag=f"A{l}s{s}_sa{cb}")
                V(lambda v, a_=a_, cb=cb: v.tensor_copy(
                    acar[:, cb:cb + 1], a_[:, SA:SA + 1]),
                  r=[a_], w=[acar], tag=f"A{l}s{s}_ac2{cb}")
                V(lambda v, b_=b_, cb=cb: v.tensor_copy(
                    b_[:, 0:1], bcar[:, cb:cb + 1]),
                  r=[bcar], w=[b_], tag=f"A{l}s{s}_bc{cb}")
                V(lambda v, b_=b_, dec=dec, ek=ek, cb=cb:
                  v.tensor_tensor_scan(b_[:, 1:SA + 1], dec[:], ek[:],
                                       bcar[:, cb:cb + 1], ALU.mult,
                                       ALU.add),
                  r=[dec, ek, bcar], w=[b_], tag=f"A{l}s{s}_sb{cb}")
                V(lambda v, b_=b_, cb=cb: v.tensor_copy(
                    bcar[:, cb:cb + 1], b_[:, SA:SA + 1]),
                  r=[b_], w=[bcar], tag=f"A{l}s{s}_bc2{cb}")
                V(lambda v, ekv=ekv, a_=a_, cb=cb:
                  v.scalar_tensor_tensor(num_s[:], ekv[:],
                                         par[("eu", l)][:, cb:cb + 1],
                                         a_[:, 0:SA], ALU.mult, ALU.add),
                  r=[ekv, a_, par[("eu", l)]], w=[num_s],
                  tag=f"A{l}s{s}_num{cb}")
                V(lambda v, ek=ek, b_=b_, cb=cb:
                  v.scalar_tensor_tensor(den_s[:], ek[:],
                                         par[("eu", l)][:, cb:cb + 1],
                                         b_[:, 0:SA], ALU.mult, ALU.add),
                  r=[ek, b_, par[("eu", l)]], w=[den_s],
                  tag=f"A{l}s{s}_den{cb}")
                V(lambda v, msc=msc: v.tensor_mul(msc[:], den_s[:],
                                                  er_s[:]),
                  r=[den_s, er_s], w=[msc], tag=f"A{l}s{s}_m1{cb}")
                V(lambda v, msc=msc: v.tensor_add(den_s[:], den_s[:],
                                                  msc[:]),
                  r=[msc, den_s], w=[den_s], tag=f"A{l}s{s}_m2{cb}")
                S(lambda e: e.activation(den_s[:], den_s[:], AF.Ln),
                  r=[den_s], w=[den_s], tag=f"A{l}s{s}_lnd{cb}")
                S(lambda e: e.activation(den_s[:], den_s[:], AF.Exp,
                                         scale=-1.0),
                  r=[den_s], w=[den_s], tag=f"A{l}s{s}_rec{cb}")
                V(lambda v, cb=cb: v.tensor_mul(rwkv_cb[cb][:], num_s[:],
                                                den_s[:]),
                  r=[num_s, den_s], w=[rwkv_cb[cb]], tag=f"A{l}s{s}_rw{cb}")
            for tb in range(ntb):
                for ob in range(NCB):
                    for half in range(2):
                        TE(lambda t, tb=tb, ob=ob, half=half: t.matmul(
                            P1024[1][:, half * 512:(half + 1) * 512],
                            rwkv_cb[ob][:, tb * 128:(tb + 1) * 128],
                            wo_sb[:, ob, half * 512:(half + 1) * 512],
                            start=(ob == 0), stop=(ob == NCB - 1)),
                           r=[rwkv_cb[ob], wo_sb], w=[P1024[1]],
                           tag=f"A{l}s{s}_wo{tb}_{ob}_{half}")
                osc = out_scr[tb % 2]
                V(lambda v, tb=tb, osc=osc: v.tensor_add(
                    osc[:], x_tm[:, tb, :], P1024[1][:]),
                  r=[x_tm, P1024[1]], w=[osc], tag=f"A{l}s{s}_res{tb}")
                DMAX(lambda e, t0=t0, tb=tb, osc=osc: e.dma_start(
                    bufA[t0 + tb * 128:t0 + (tb + 1) * 128, :], osc[:]),
                     r=[osc], w=[("bufA", t0 + tb)], tag=f"A{l}s{s}_st{tb}")
        sch.alias([("bufA", s * SA + tb) for s in range(T // SA)
                   for tb in range(SA // 128)], ["bufA"])

    def phase_B(l, src, dst_dram, dst_name, is_ffn=False, sandwich=False):
        ntb = SB // 128
        if not is_ffn:
            for cb in range(NCB):
                V(lambda v, cb=cb: v.memset(h_cb[cb][:, 0:1], 0.0),
                  w=[h_cb[cb]], tag=f"B{l}_h0_{cb}")
        wstream = w2_bf if is_ffn else wcv_bf[l]
        for s in range(T // SB):
            t0 = s * SB
            DMAX(lambda e, t0=t0, src=src: e.dma_start(
                x_tm[:, 0:ntb, :], src[t0:t0 + SB, :].rearrange(
                    "(tb p) c -> p tb c", p=128)),
                 r=[dkey(src)], w=[x_tm], tag=f"B{l}s{s}_ldx")
            if sandwich:
                DMAX(lambda e, t0=t0: e.dma_start(
                    xaux_tm[:], x0[t0:t0 + SB, :].rearrange(
                        "(tb p) c -> p tb c", p=128)),
                     r=[dkey(x0)], w=[xaux_tm], tag=f"B{l}s{s}_ldx0")
            xt = [x_tm[:, tb, :] for tb in range(ntb)]
            xtk = [x_tm] * ntb
            stats_ln(xt, xtk, ntb, 0, f"B{l}s{s}_l2")
            for tb in range(ntb):
                V(lambda v, tb=tb: v.tensor_scalar(
                    xn[tb][:], x_tm[:, tb, :], smean[:, tb:tb + 1],
                    srstd[:, tb:tb + 1], ALU.subtract, ALU.mult),
                  r=[x_tm, smean, srstd], w=[xn[tb]], tag=f"B{l}s{s}_xn{tb}")
            for cb in range(NCB):
                pt = P512[cb % 2]
                for tb in range(ntb):
                    TE(lambda t, tb=tb, cb=cb, pt=pt: t.transpose(
                        pt[:, tb * 128:(tb + 1) * 128],
                        xn[tb][:, cb * 128:(cb + 1) * 128], idf[:]),
                       r=[xn[tb], idf], w=[pt], tag=f"B{l}s{s}_tr{cb}_{tb}")
                if is_ffn:
                    S(lambda e, cb=cb, pt=pt: e.activation(
                        xk_cb[cb][:, 0:SB], pt[:, 0:SB], AF.Identity),
                      r=[pt], w=[xk_cb[cb]], tag=f"B{l}s{s}_ev{cb}")
                else:
                    S(lambda e, cb=cb, pt=pt: e.activation(
                        h_cb[cb][:, 1:SB + 1], pt[:, 0:SB], AF.Identity,
                        bias=par[("ln2b", l)][:, cb:cb + 1],
                        scale=par[("ln2w", l)][:, cb:cb + 1]),
                      r=[pt, par[("ln2w", l)], par[("ln2b", l)]],
                      w=[h_cb[cb]], tag=f"B{l}s{s}_ev{cb}")
                    dsc = dscB_s[cb % 2]
                    V(lambda v, cb=cb, dsc=dsc: v.tensor_sub(
                        dsc[:, 0:SB], h_cb[cb][:, 1:SB + 1],
                        h_cb[cb][:, 0:SB]),
                      r=[h_cb[cb]], w=[dsc], tag=f"B{l}s{s}_d{cb}")
                    for nm, dst in (("cmk", xk_cb), ("cmr", xr_cb)):
                        V(lambda v, cb=cb, nm=nm, dst=dst, dsc=dsc:
                          v.scalar_tensor_tensor(
                              dst[cb][:, 0:SB], dsc[:, 0:SB],
                              par[(nm, l)][:, cb:cb + 1],
                              h_cb[cb][:, 0:SB], ALU.mult, ALU.add),
                          r=[dsc, h_cb[cb], par[(nm, l)]], w=[dst[cb]],
                          tag=f"B{l}s{s}_m{nm}{cb}")
                    V(lambda v, cb=cb: v.tensor_copy(
                        h_cb[cb][:, 0:1], h_cb[cb][:, SB:SB + 1]),
                      r=[h_cb[cb]], w=[h_cb[cb]], tag=f"B{l}s{s}_bnd{cb}")
            for g in range(NFB // 4):
                wb = wcvb_s[g % 2]
                DMAW(lambda gp, g=g, wb=wb, ws=wstream: gp.dma_start(
                    wb[:], ws[g * 512:(g + 1) * 512, :].rearrange(
                        "(fb p) c -> p fb c", p=128)),
                     w=[wb], tag=f"B{l}s{s}_wld{g}")
                for fi in range(4):
                    fb = g * 4 + fi
                    pkf = P512[2 + fb % 2]
                    for ci in range(NCB):
                        TE(lambda t, fb=fb, ci=ci, pkf=pkf: t.matmul(
                            pkf[:, 0:SB], wck_sb[:, ci, fb, :],
                            xk_cb[ci][:, 0:SB],
                            start=(ci == 0), stop=(ci == NCB - 1)),
                           r=[wck_sb, xk_cb[ci]], w=[pkf],
                           tag=f"B{l}s{s}_ck{fb}_{ci}")
                    kf = kf_s[fb % 2]
                    if is_ffn:
                        S(lambda e, pkf=pkf, kf=kf, fb=fb: e.activation(
                            kf[:], pkf[:, 0:SB], AF.Gelu,
                            bias=b1_sb[:, fb:fb + 1]),
                          r=[pkf, b1_sb], w=[kf], tag=f"B{l}s{s}_gel{fb}")
                    else:
                        S(lambda e, pkf=pkf: e.activation(
                            rl_s[:], pkf[:, 0:SB], AF.Relu),
                          r=[pkf], w=[rl_s], tag=f"B{l}s{s}_rl{fb}")
                        S(lambda e, kf=kf: e.activation(
                            kf[:], rl_s[:], AF.Square),
                          r=[rl_s], w=[kf], tag=f"B{l}s{s}_sq{fb}")
                    for tb in range(ntb):
                        for half in range(2):
                            TE(lambda t, fb=fb, fi=fi, tb=tb, half=half,
                               kf=kf, wb=wb: t.matmul(
                                   P1024[tb][:,
                                             half * 512:(half + 1) * 512],
                                   kf[:, tb * 128:(tb + 1) * 128],
                                   wb[:, fi, half * 512:(half + 1) * 512],
                                   start=(fb == 0), stop=(fb == NFB - 1)),
                               r=[kf, wb], w=[P1024[tb]],
                               tag=f"B{l}s{s}_cv{fb}_{tb}_{half}")
            for tb in range(ntb):
                osc = out_scr[tb % 2]
                gj = gj_s[tb % 2]
                if is_ffn:
                    V(lambda v, tb=tb, gj=gj: v.tensor_add(
                        gj[:], P1024[tb][:], b2b[:]),
                      r=[P1024[tb], b2b], w=[gj], tag=f"B{l}s{s}_b2{tb}")
                else:
                    for half in range(2):
                        pg = P512[2 + half]
                        for ci in range(NCB):
                            TE(lambda t, tb=tb, half=half, ci=ci, pg=pg:
                               t.matmul(
                                   pg[:],
                                   xr_cb[ci][:, tb * 128:(tb + 1) * 128],
                                   wcr_sb[:, ci,
                                          half * 512:(half + 1) * 512],
                                   start=(ci == 0), stop=(ci == NCB - 1)),
                               r=[xr_cb[ci], wcr_sb], w=[pg],
                               tag=f"B{l}s{s}_cr{tb}_{half}_{ci}")
                        S(lambda e, half=half, pg=pg, gj=gj: e.activation(
                            gj[:, half * 512:(half + 1) * 512], pg[:],
                            AF.Exp, scale=-1.0),
                          r=[pg], w=[gj], tag=f"B{l}s{s}_er{tb}_{half}")
                    S(lambda e, gj=gj: e.activation(gj[:], gj[:], AF.Ln,
                                                    bias=cst[:, 1:2]),
                      r=[gj, cst], w=[gj], tag=f"B{l}s{s}_l1p{tb}")
                    S(lambda e, gj=gj: e.activation(gj[:], gj[:], AF.Exp,
                                                    scale=-1.0),
                      r=[gj], w=[gj], tag=f"B{l}s{s}_sg{tb}")
                    V(lambda v, tb=tb, gj=gj: v.tensor_mul(
                        gj[:], gj[:], P1024[tb][:]),
                      r=[P1024[tb], gj], w=[gj], tag=f"B{l}s{s}_gkv{tb}")
                if sandwich:
                    V(lambda v, tb=tb, gj=gj: v.tensor_add(
                        gj[:], gj[:], x_tm[:, tb, :]),
                      r=[x_tm, gj], w=[gj], tag=f"B{l}s{s}_xb{tb}")
                elif is_ffn:
                    pass
                else:
                    V(lambda v, tb=tb, osc=osc, gj=gj: v.tensor_add(
                        osc[:], gj[:], x_tm[:, tb, :]),
                      r=[x_tm, gj], w=[osc], tag=f"B{l}s{s}_res{tb}")
                    DMAX(lambda e, t0=t0, tb=tb, osc=osc, d=dst_dram:
                         e.dma_start(
                             d[t0 + tb * 128:t0 + (tb + 1) * 128, :],
                             osc[:]),
                         r=[osc], w=[(dst_name, t0 + tb)],
                         tag=f"B{l}s{s}_st{tb}")
            if sandwich or is_ffn:
                gt = [gj_s[tb % 2][:] for tb in range(ntb)]
                gtk = [gj_s[tb % 2] for tb in range(ntb)]
                stats_ln(gt, gtk, ntb, 4, f"B{l}s{s}_f")
                for tb in range(ntb):
                    osc = out_scr[tb % 2]
                    V(lambda v, tb=tb, osc=osc: v.tensor_scalar(
                        osc[:], gj_s[tb % 2][:], smean[:, 4 + tb:5 + tb],
                        srstd[:, 4 + tb:5 + tb], ALU.subtract, ALU.mult),
                      r=[gj_s[tb % 2], smean, srstd], w=[osc],
                      tag=f"B{l}s{s}_fn{tb}")
                    if sandwich:
                        V(lambda v, tb=tb, osc=osc: v.tensor_add(
                            osc[:], osc[:], xaux_tm[:, tb, :]),
                          r=[xaux_tm, osc], w=[osc], tag=f"B{l}s{s}_fa{tb}")
                    else:
                        V(lambda v, tb=tb, osc=osc: v.tensor_add(
                            osc[:], osc[:], x_tm[:, tb, :]),
                          r=[x_tm, osc], w=[osc], tag=f"B{l}s{s}_fa{tb}")
                    DMAX(lambda e, t0=t0, tb=tb, osc=osc, d=dst_dram:
                         e.dma_start(
                             d[t0 + tb * 128:t0 + (tb + 1) * 128, :],
                             osc[:]),
                         r=[osc], w=[(dst_name, t0 + tb)],
                         tag=f"B{l}s{s}_st{tb}")
        sch.alias([(dst_name, s * SB + tb) for s in range(T // SB)
                   for tb in range(SB // 128)], [dst_name])

    cur = x0
    for l in range(n_layers):
        sch.alias(RW1_B, RW1_A)
        sch.alias(RW2_B, RW2_A)
        load_weights_A(l)
        phase_A(l, cur)
        sch.alias(RW1_A, RW1_B)
        sch.alias(RW2_A, RW2_B)
        if l == 0:
            sch.alias(RW3_P, RW3_B)
        load_weights_B(l)
        sandwich = (l == L - 1) and (n_layers == L) and with_ffn
        phase_B(l, bufA, bufB, "bufB", is_ffn=False, sandwich=sandwich)
        cur = bufB
    if with_ffn and n_layers == L:
        sch.alias(RW1_A, RW1_B)
        sch.alias(RW3_B, RW3_F)
        load_weights_F()
        DMAX(lambda e: e.dma_start(out_scr[1][0:1, :], b2_d),
             w=[out_scr[1]], tag="row_b2")
        for half in range(2):
            TE(lambda t, half=half: t.matmul(
                P1024[0][:, half * 512:(half + 1) * 512], ones_col[:],
                out_scr[1][0:1, half * 512:(half + 1) * 512],
                start=True, stop=True),
               r=[ones_col, out_scr[1]], w=[P1024[0]], tag=f"bc_b2{half}")
        V(lambda v: v.tensor_copy(b2b[:], P1024[0][:]),
          r=[P1024[0]], w=[b2b], tag="bcc_b2")
        phase_B(-1, bufB, out_d, "out", is_ffn=True, sandwich=False)
    else:
        if dump == "A":
            cur = bufA
        for s in range(T // SA):
            t0 = s * SA
            DMAX(lambda e, t0=t0, src=cur: e.dma_start(
                x_tm[:], src[t0:t0 + SA, :].rearrange(
                    "(tb p) c -> p tb c", p=128)),
                 r=[dkey(cur)], w=[x_tm], tag=f"cp{s}_ld")
            DMAX(lambda e, t0=t0: e.dma_start(
                out_d[t0:t0 + SA, :].rearrange("(tb p) c -> p tb c", p=128),
                x_tm[:]),
                 r=[x_tm], w=[("out", s)], tag=f"cp{s}_st")

    sch.finalize()

    with nc.Block() as blk:
        @blk.tensor
        def _(eng):
            sch.emit("tensor", eng)

        @blk.vector
        def _(eng):
            sch.emit("vector", eng)

        @blk.scalar
        def _(eng):
            sch.emit("scalar", eng)

        @blk.gpsimd
        def _(eng):
            sch.emit("gpsimd", eng)

        @blk.sync
        def _(eng):
            sch.emit("sync", eng)
            eng.wait_ge(sch.sems["sync"], sch.totals["sync"])
            eng.wait_ge(sch.sems["gpsimd"], sch.totals["gpsimd"])

    return nc


def make_inputs(inputs, n_layers=L):
    def row128(v):
        return np.ascontiguousarray(np.asarray(v).reshape(NCB, 128).T,
                                    dtype=np.float32)

    base = {
        "idf": np.eye(128, dtype=np.float32),
        "ln0w": np.ascontiguousarray(
            np.asarray(inputs["L0_attn_ln0_w"], dtype=np.float32)[None, :]),
        "ln0b": np.ascontiguousarray(
            np.asarray(inputs["L0_attn_ln0_b"], dtype=np.float32)[None, :]),
        "w1": np.ascontiguousarray(inputs["L0_ffn_w1"], dtype=np.float32),
        "b1": np.ascontiguousarray(
            np.asarray(inputs["L0_ffn_b1"],
                       dtype=np.float32).reshape(NFB, 128).T),
        "w2": np.ascontiguousarray(inputs["L0_ffn_w2"], dtype=np.float32),
        "b2": np.ascontiguousarray(
            np.asarray(inputs["L0_ffn_b2"], dtype=np.float32)[None, :]),
    }
    for l in range(L):
        base[f"wkT{l}"] = np.ascontiguousarray(
            np.asarray(inputs["L0_attn_Wk"][l], dtype=np.float32).T)
        base[f"wvT{l}"] = np.ascontiguousarray(
            np.asarray(inputs["L0_attn_Wv"][l], dtype=np.float32).T)
        base[f"wrT{l}"] = np.ascontiguousarray(
            np.asarray(inputs["L0_attn_Wr"][l], dtype=np.float32).T)
        base[f"woT{l}"] = np.ascontiguousarray(
            np.asarray(inputs["L0_attn_Wo"][l], dtype=np.float32).T)
        base[f"wckT{l}"] = np.ascontiguousarray(
            np.asarray(inputs["L0_attn_Wck"][l], dtype=np.float32).T)
        base[f"wcvT{l}"] = np.ascontiguousarray(
            np.asarray(inputs["L0_attn_Wcv"][l], dtype=np.float32).T)
        base[f"wcrT{l}"] = np.ascontiguousarray(
            np.asarray(inputs["L0_attn_Wcr"][l], dtype=np.float32).T)
        base[f"tmk{l}"] = row128(inputs["L0_attn_tmk"][l])
        base[f"tmv{l}"] = row128(inputs["L0_attn_tmv"][l])
        base[f"tmr{l}"] = row128(inputs["L0_attn_tmr"][l])
        base[f"cmk{l}"] = row128(inputs["L0_attn_cmk"][l])
        base[f"cmr{l}"] = row128(inputs["L0_attn_cmr"][l])
        base[f"td{l}"] = row128(inputs["L0_attn_time_decay"][l])
        base[f"tf{l}"] = row128(inputs["L0_attn_time_first"][l])
        base[f"ln1w{l}"] = row128(inputs["L0_attn_ln1_w"][l])
        base[f"ln1b{l}"] = row128(inputs["L0_attn_ln1_b"][l])
        base[f"ln2w{l}"] = row128(inputs["L0_attn_ln2_w"][l])
        base[f"ln2b{l}"] = row128(inputs["L0_attn_ln2_b"][l])
    return base


def kernel(**inputs):
    n = 8
    x = np.ascontiguousarray(inputs["L0_attn_x"], dtype=np.float32)
    base = make_inputs(inputs)
    nc = build_nc()
    in_maps = []
    for b in range(n):
        m = dict(base)
        m["x0"] = np.ascontiguousarray(x[b])
        in_maps.append(m)
    res = run_bass_kernel_spmd(nc, in_maps, core_ids=list(range(n)))
    return np.stack([r["out"] for r in res.results], axis=0)
